# version 02 (final)
# baseline (speedup 1.0000x reference)
import numpy as np
import concourse.bass as bass
import concourse.mybir as mybir
from concourse.bass_utils import run_bass_kernel_spmd

F32 = mybir.dt.float32
BF16 = mybir.dt.bfloat16
AF = mybir.ActivationFunctionType
ALU = mybir.AluOpType
AX = mybir.AxisListType

S = 8192
D = 1024
NH = 16
HD = 64
NC8 = 8
NSC = 16
NCC = 64
E = 8
F = 4096
FCH = 512
NFC = F // FCH
NFH = FCH // 128


def build_nc():
    nc = bass.Bass("TRN2", target_bir_lowering=False)
    x = nc.dram_tensor("x", [S, D], F32, kind="ExternalInput").ap()
    Wq = nc.dram_tensor("Wq", [D, D], F32, kind="ExternalInput").ap()
    Wk = nc.dram_tensor("Wk", [D, D], F32, kind="ExternalInput").ap()
    Wv = nc.dram_tensor("Wv", [D, D], F32, kind="ExternalInput").ap()
    Wbeta = nc.dram_tensor("Wbeta", [D, NH], F32, kind="ExternalInput").ap()
    bbeta = nc.dram_tensor("bbeta", [NH, 1], F32, kind="ExternalInput").ap()
    Wo = nc.dram_tensor("Wo", [D, D], F32, kind="ExternalInput").ap()
    bo = nc.dram_tensor("bo", [NC8, 128, 1], F32, kind="ExternalInput").ap()
    lng = nc.dram_tensor("lng", [NC8, 128, 1], F32, kind="ExternalInput").ap()
    lnb = nc.dram_tensor("lnb", [NC8, 128, 1], F32, kind="ExternalInput").ap()
    gw = nc.dram_tensor("gw", [E, D], F32, kind="ExternalInput").ap()
    wig = nc.dram_tensor("wig", [E, F, D], F32, kind="ExternalInput").ap()
    wiu = nc.dram_tensor("wiu", [E, F, D], F32, kind="ExternalInput").ap()
    wo_w = nc.dram_tensor("wo_w", [E, D, F], F32, kind="ExternalInput").ap()
    I128f = nc.dram_tensor("I128f", [128, 128], F32, kind="ExternalInput").ap()
    I128b = nc.dram_tensor("I128b", [128, 128], BF16, kind="ExternalInput").ap()
    UTb = nc.dram_tensor("UTb", [128, 128], BF16, kind="ExternalInput").ap()
    mneg = nc.dram_tensor("mneg", [128, 2048], BF16, kind="ExternalInput").ap()
    STb = nc.dram_tensor("STb", [NH, 1024], BF16, kind="ExternalInput").ap()
    Sb = nc.dram_tensor("Sb", [128, 128], BF16, kind="ExternalInput").ap()
    ones_f = nc.dram_tensor("ones_f", [128, 128], F32, kind="ExternalInput").ap()
    s_hbm = nc.dram_tensor("s_hbm", [NC8, 128, S], BF16, kind="Internal").ap()
    gacc = nc.dram_tensor("gacc", [NC8, 128, S], BF16, kind="Internal").ap()
    w8h = nc.dram_tensor("w8h", [E, S], F32, kind="Internal").ap()
    y_out = nc.dram_tensor("y", [S, D], F32, kind="ExternalOutput").ap()

    import contextlib
    st = contextlib.ExitStack()
    nc.__enter__ if False else None

    SEMN = 16
    sems = [st.enter_context(nc.semaphore(name=f"gsem{i}"))
            for i in range(SEMN)]
    state = {"i": 0, "v": 0, "pf": []}
    CAP = 60000

    def w(eng):
        if state["v"] == 0 and state["i"] > 0:
            eng.wait_ge(sems[state["i"] - 1], state["pf"][-1])
        eng.wait_ge(sems[state["i"]], state["v"])

    def inc(instr, n=1):
        instr.then_inc(sems[state["i"]], n)
        state["v"] += n
        if state["v"] >= CAP:
            state["pf"].append(state["v"])
            state["i"] += 1
            state["v"] = 0

    def mm(out, lhsT, rhs, start, stop, n=1, wait=True):
        if wait:
            w(nc.tensor)
        ins = nc.tensor.matmul(out, lhsT, rhs, start=start, stop=stop)
        if n:
            inc(ins, n)

    def tp(out, in_, ident):
        w(nc.tensor)
        inc(nc.tensor.transpose(out, in_, ident))

    def cp(out, in_):
        w(nc.scalar)
        inc(nc.scalar.copy(out, in_))

    def act(out, in_, f, bias=None, scale=None):
        w(nc.scalar)
        kw = {}
        if bias is not None:
            kw["bias"] = bias
        if scale is not None:
            kw["scale"] = scale
        inc(nc.scalar.activation(out, in_, f, **kw))

    def tt(out, a, b, op):
        w(nc.vector)
        inc(nc.vector.tensor_tensor(out, a, b, op))

    def ts(out, a, s_, op):
        w(nc.vector)
        inc(nc.vector.tensor_scalar(out, a, s_, None, op0=op))

    def dma(out, in_, slow=False):
        w(nc.sync)
        kw = {"allow_slow_non_contiguous": True} if slow else {}
        inc(nc.sync.dma_start(out, in_, **kw), 16)

    xnT = st.enter_context(nc.sbuf_tensor([128, NC8, S], BF16))
    idf = st.enter_context(nc.sbuf_tensor([128, 128], F32))
    idb = st.enter_context(nc.sbuf_tensor([128, 128], BF16))
    utb = st.enter_context(nc.sbuf_tensor([128, 128], BF16))
    mng = st.enter_context(nc.sbuf_tensor([128, 2048], BF16))
    stb = st.enter_context(nc.sbuf_tensor([NH, 1024], BF16))
    sbm = st.enter_context(nc.sbuf_tensor([128, 128], BF16))
    onef = st.enter_context(nc.sbuf_tensor([128, 128], F32))
    bbc = st.enter_context(nc.sbuf_tensor([NH, 1], F32))
    boc = st.enter_context(nc.sbuf_tensor([128, NC8], F32))
    lngc = st.enter_context(nc.sbuf_tensor([128, NC8], F32))
    lnbc = st.enter_context(nc.sbuf_tensor([128, NC8], F32))
    wq1 = st.enter_context(nc.sbuf_tensor([128, 128], BF16))
    wv1 = st.enter_context(nc.sbuf_tensor([128, 128], BF16))
    wbb = st.enter_context(nc.sbuf_tensor([128, 128], BF16))
    gwT = st.enter_context(nc.sbuf_tensor([128, 64], BF16))
    stateb = st.enter_context(nc.sbuf_tensor([1, D], BF16))
    epsc = st.enter_context(nc.sbuf_tensor([128, 1], F32))

    dma(idf[:, :], I128f)
    dma(idb[:, :], I128b)
    dma(utb[:, :], UTb)
    dma(mng[:, :], mneg)
    dma(stb[:, :], STb)
    dma(sbm[:, :], Sb)
    dma(onef[:, :], ones_f)
    dma(bbc[:, :], bbeta, slow=True)
    for c in range(NC8):
        dma(boc[:, c:c + 1], bo[c], slow=True)
        dma(lngc[:, c:c + 1], lng[c], slow=True)
        dma(lnbc[:, c:c + 1], lnb[c], slow=True)
    w(nc.vector)
    inc(nc.vector.memset(stateb[:, :], 0.0))
    w(nc.vector)
    inc(nc.vector.memset(epsc[:, :], 1e-5))

    with nc.sbuf_tensor([128, 1024], F32) as stg, \
            nc.sbuf_tensor([128, 1024], BF16) as stgb, \
            nc.sbuf_tensor([128, 1024], BF16) as wTc, \
            nc.sbuf_tensor([16, 1024], BF16) as w1T:
        for (Wsrc, wdst) in ((Wq, wq1), (Wv, wv1)):
            with nc.psum_tensor([16, 512], F32) as p1a, \
                    nc.psum_tensor([16, 512], F32) as p1b:
                for jc in range(NC8):
                    with nc.psum_tensor([128, 1024], BF16) as ptp:
                        for rc in range(NC8):
                            dma(stg[:, 0:128],
                                Wsrc[rc * 128:(rc + 1) * 128,
                                     jc * 128:(jc + 1) * 128])
                            cp(stgb[:, 0:128], stg[:, 0:128])
                            tp(ptp[:, rc * 128:(rc + 1) * 128],
                               stgb[:, 0:128], idb[:, :])
                        cp(wTc[:, :], ptp[:, :])
                    mm(p1a[:, :], sbm[:, jc * 16:(jc + 1) * 16],
                       wTc[:, 0:512], jc == 0, jc == NC8 - 1)
                    mm(p1b[:, :], sbm[:, jc * 16:(jc + 1) * 16],
                       wTc[:, 512:1024], jc == 0, jc == NC8 - 1)
                cp(w1T[:, 0:512], p1a[:, :])
                cp(w1T[:, 512:1024], p1b[:, :])
            with nc.psum_tensor([128, 128], BF16) as ptp2:
                for c in range(NC8):
                    tp(ptp2[:, c * 16:(c + 1) * 16],
                       w1T[:, c * 128:(c + 1) * 128], idb[0:16, 0:16])
                cp(wdst[:, :], ptp2[:, :])
        for c in range(NC8):
            dma(stg[:, 0:16], Wbeta[c * 128:(c + 1) * 128, :])
            cp(wbb[:, c * 16:(c + 1) * 16], stg[:, 0:16])
        with nc.sbuf_tensor([E, 1024], F32) as gstg, \
                nc.sbuf_tensor([E, 1024], BF16) as gstgb, \
                nc.psum_tensor([128, 64], BF16) as pgt:
            dma(gstg[:, :], gw)
            cp(gstgb[:, :], gstg[:, :])
            for c in range(NC8):
                tp(pgt[:, c * 8:(c + 1) * 8],
                   gstgb[:, c * 128:(c + 1) * 128], idb[0:8, 0:8])
            cp(gwT[:, :], pgt[:, :])

    if True:
        ph = contextlib.ExitStack()
        with ph:
            wkob = ph.enter_context(nc.sbuf_tensor([128, 8192], BF16))
            usb = ph.enter_context(nc.sbuf_tensor([128, NC8, 512], BF16))
            qs = ph.enter_context(nc.sbuf_tensor([16, 512], BF16))
            vs = ph.enter_context(nc.sbuf_tensor([16, 512], BF16))
            lb = ph.enter_context(nc.sbuf_tensor([16, 512], BF16))
            xt = ph.enter_context(nc.sbuf_tensor([128, 1024], F32))
            vbs = ph.enter_context(nc.sbuf_tensor([128, 512], F32))
            xnb = ph.enter_context(nc.sbuf_tensor([128, 1024], BF16))
            r1 = ph.enter_context(nc.sbuf_tensor([128, 1], F32))
            r2 = ph.enter_context(nc.sbuf_tensor([128, 1], F32))
            Aall = ph.enter_context(nc.sbuf_tensor([128, 2048], F32))
            DTb = ph.enter_context(nc.sbuf_tensor([128, 2048], BF16))
            clbtm = ph.enter_context(nc.sbuf_tensor([128, 128], F32))
            clbT = ph.enter_context(nc.sbuf_tensor([16, 128], F32))
            rowa = ph.enter_context(nc.sbuf_tensor([1, 2048], F32))
            erow = ph.enter_context(nc.sbuf_tensor([1, 2048], BF16))
            lbtm = ph.enter_context(nc.sbuf_tensor([128, 16], BF16))
            utm = ph.enter_context(nc.sbuf_tensor([128, 1024], BF16))
            qtm = ph.enter_context(nc.sbuf_tensor([128, 16], F32))
            oT = ph.enter_context(nc.sbuf_tensor([128, 1024], BF16))
            mrow = ph.enter_context(nc.sbuf_tensor([1, 128], F32))
            vrow = ph.enter_context(nc.sbuf_tensor([1, 128], F32))
            rrow = ph.enter_context(nc.sbuf_tensor([1, 128], F32))
            mrrow = ph.enter_context(nc.sbuf_tensor([1, 128], F32))
            ysq = Aall[:, 0:1024]
            tA = Aall[:, 1024:2048]
            otm = xnb
            yT = xt
            sTt = utm
            for sc in range(NSC):
                for c in range(NC8):
                    dma(xt[:, :], Wk[c * 128:(c + 1) * 128, :])
                    cp(wkob[:, c * 1024:(c + 1) * 1024], xt[:, :])
                for cc in range(4):
                    g0 = sc * 512 + cc * 128
                    dma(xt[:, :], x[g0:g0 + 128, :])
                    w(nc.vector)
                    inc(nc.vector.reduce_sum(r1[:, :], xt[:, :], axis=AX.X))
                    act(r1[:, :], r1[:, :], AF.Copy, scale=1.0 / D)
                    ts(xt[:, :], xt[:, :], r1[:, :], ALU.subtract)
                    act(ysq[:, :], xt[:, :], AF.Square)
                    w(nc.vector)
                    inc(nc.vector.reduce_sum(r2[:, :], ysq[:, :], axis=AX.X))
                    act(r2[:, :], r2[:, :], AF.Sqrt, scale=1.0 / D,
                        bias=epsc[:, :])
                    w(nc.vector)
                    inc(nc.vector.reciprocal(r2[:, :], r2[:, :]))
                    ts(xnb[:, :], xt[:, :], r2[:, :], ALU.mult)
                    with nc.psum_tensor([128, 1024], BF16) as ptp:
                        for c in range(NC8):
                            tp(ptp[:, c * 128:(c + 1) * 128],
                               xnb[:, c * 128:(c + 1) * 128], idb[:, :])
                        cp(xnT[:, :, g0:g0 + 128],
                           ptp[:, :].rearrange("p (c t) -> p c t", c=NC8))
                t0 = sc * 512
                with nc.psum_tensor([16, 512], F32) as p16:
                    for (wsrc, dst) in ((wq1, qs), (wv1, vs)):
                        for c in range(NC8):
                            mm(p16[:, :], wsrc[:, c * 16:(c + 1) * 16],
                               xnT[:, c, t0:t0 + 512], c == 0, c == NC8 - 1)
                        cp(dst[:, :], p16[:, :])
                    for c in range(NC8):
                        mm(p16[:, :], wbb[:, c * 16:(c + 1) * 16],
                           xnT[:, c, t0:t0 + 512], c == 0, c == NC8 - 1)
                    act(Aall[0:16, 0:512], p16[:, :], AF.Sigmoid,
                        bias=bbc[:, :])
                    act(lb[:, :], Aall[0:16, 0:512], AF.Ln)
                with nc.psum_tensor([128, 512], F32) as pk, \
                        nc.psum_tensor([128, 512], F32) as pvb:
                    for o in range(NC8):
                        mm(pvb[:, :], stb[:, o * 128:(o + 1) * 128], vs[:, :],
                           True, True)
                        cp(vbs[:, :], pvb[:, :])
                        for c in range(NC8):
                            mm(pk[:, :],
                               wkob[:, c * 1024 + o * 128:c * 1024 + (o + 1) * 128],
                               xnT[:, c, t0:t0 + 512], c == 0, c == NC8 - 1)
                        tt(usb[:, o, :], vbs[:, :], pk[:, :], ALU.mult)
                for c in range(NC8):
                    dma(xt[:, :], Wo[c * 128:(c + 1) * 128, :])
                    cp(wkob[:, c * 1024:(c + 1) * 1024], xt[:, :])
                for cc in range(4):
                    g0 = t0 + cc * 128
                    tl = slice(cc * 128, (cc + 1) * 128)
                    with nc.psum_tensor([128, 16], BF16) as plb:
                        tp(plb[:, :], lb[:, tl], idb[0:16, 0:16])
                        cp(lbtm[:, :], plb[:, :])
                    with nc.psum_tensor([128, 16], F32) as pcl:
                        mm(pcl[:, :], utb[:, :], lbtm[:, :], True, True)
                        cp(clbtm[:, 0:16], pcl[:, :])
                    with nc.psum_tensor([16, 128], F32) as pct:
                        tp(pct[:, :], clbtm[:, 0:16], idf[:, :])
                        cp(clbT[:, :], pct[0:16, :])
                    dma(rowa[0:1, :].rearrange("a (h t) -> a h t", h=16),
                        clbT[:, :])
                    act(erow[:, :], rowa[:, :], AF.Exp)
                    with nc.psum_tensor([128, 2048], F32) as pR:
                        for q4 in range(4):
                            sl = slice(q4 * 512, (q4 + 1) * 512)
                            mm(pR[:, sl], onef[0:1, :], rowa[:, sl], True, True)
                        for h in range(NH):
                            hs = slice(h * 128, (h + 1) * 128)
                            ts(Aall[:, hs], pR[:, hs], clbtm[:, h:h + 1],
                               ALU.subtract)
                    tt(Aall[:, :], Aall[:, :], mng[:, :], ALU.add)
                    act(DTb[:, :], Aall[:, :], AF.Exp)
                    with nc.psum_tensor([128, 1024], BF16) as pu:
                        for h in range(NH):
                            r0 = (h % 2) * 64
                            tp(pu[:, h * 64:(h + 1) * 64],
                               usb[r0:r0 + 64, h // 2, tl],
                               idb[r0:r0 + 64, r0:r0 + 64])
                        cp(utm[:, :], pu[:, :])
                    with nc.psum_tensor([128, 1024], F32) as pC, \
                            nc.psum_tensor([1, 512], F32) as pS1, \
                            nc.psum_tensor([1, 512], F32) as pS2:
                        for h in range(NH):
                            cs = slice(h * 64, (h + 1) * 64)
                            pS = pS1 if h < 8 else pS2
                            ps_cs = slice((h % 8) * 64, (h % 8 + 1) * 64)
                            hs = slice(h * 128, (h + 1) * 128)
                            mm(pC[:, cs], DTb[:, hs], utm[:, cs], True, False)
                            mm(pC[:, cs], erow[0:1, hs], stateb[0:1, cs],
                               False, True)
                            mm(pS[0:1, ps_cs],
                               DTb[:, h * 128 + 127:h * 128 + 128],
                               utm[:, cs], True, False)
                            mm(pS[0:1, ps_cs],
                               erow[0:1, h * 128 + 127:h * 128 + 128],
                               stateb[0:1, cs], False, True)
                        with nc.psum_tensor([128, 16], BF16) as pq:
                            tp(pq[:, :], qs[:, tl], idb[0:16, 0:16])
                            cp(qtm[:, :], pq[:, :])
                        for h in range(NH):
                            cs = slice(h * 64, (h + 1) * 64)
                            ts(otm[:, cs], pC[:, cs], qtm[:, h:h + 1], ALU.mult)
                        cp(stateb[:, 0:512], pS1[:, :])
                        cp(stateb[:, 512:1024], pS2[:, :])
                    with nc.psum_tensor([128, 1024], BF16) as po:
                        for c in range(NC8):
                            tp(po[:, c * 128:(c + 1) * 128],
                               otm[:, c * 128:(c + 1) * 128], idb[:, :])
                        cp(oT[:, :], po[:, :])
                    with nc.psum_tensor([128, 1024], F32) as pw:
                        for o in range(NC8):
                            for c in range(NC8):
                                mm(pw[:, o * 128:(o + 1) * 128],
                                   wkob[:, c * 1024 + o * 128:c * 1024 + (o + 1) * 128],
                                   oT[:, c * 128:(c + 1) * 128],
                                   c == 0, c == NC8 - 1)
                        for o in range(NC8):
                            os_ = slice(o * 128, (o + 1) * 128)
                            tt(yT[:, os_], xnT[:, o, g0:g0 + 128], pw[:, os_],
                               ALU.add)
                            ts(yT[:, os_], yT[:, os_], boc[:, o:o + 1], ALU.add)
                    act(ysq[:, :], yT[:, :], AF.Square)
                    with nc.psum_tensor([1, 128], F32) as ps1, \
                            nc.psum_tensor([1, 128], F32) as ps2:
                        for o in range(NC8):
                            os_ = slice(o * 128, (o + 1) * 128)
                            mm(ps1[:, :], onef[:, 0:1], yT[:, os_],
                               o == 0, o == NC8 - 1)
                        for o in range(NC8):
                            os_ = slice(o * 128, (o + 1) * 128)
                            mm(ps2[:, :], onef[:, 0:1], ysq[:, os_],
                               o == 0, o == NC8 - 1)
                        act(mrow[:, :], ps1[:, :], AF.Copy, scale=1.0 / D)
                        act(vrow[:, :], ps2[:, :], AF.Copy, scale=1.0 / D)
                    act(rrow[:, :], mrow[:, :], AF.Square)
                    tt(vrow[:, :], vrow[:, :], rrow[:, :], ALU.subtract)
                    act(vrow[:, :], vrow[:, :], AF.Sqrt, bias=epsc[0:1, :])
                    w(nc.vector)
                    inc(nc.vector.reciprocal(rrow[:, :], vrow[:, :]))
                    tt(mrrow[:, :], mrow[:, :], rrow[:, :], ALU.mult)
                    with nc.psum_tensor([128, 256], F32) as pb:
                        mm(pb[:, 0:128], onef[0:1, :], rrow[:, :], True, True)
                        mm(pb[:, 128:256], onef[0:1, :], mrrow[:, :],
                           True, True)
                        for o in range(NC8):
                            os_ = slice(o * 128, (o + 1) * 128)
                            tt(tA[:, os_], yT[:, os_], pb[:, 0:128], ALU.mult)
                            tt(tA[:, os_], tA[:, os_], pb[:, 128:256],
                               ALU.subtract)
                        for o in range(NC8):
                            os_ = slice(o * 128, (o + 1) * 128)
                            ts(tA[:, os_], tA[:, os_], lngc[:, o:o + 1],
                               ALU.mult)
                            ts(sTt[:, os_], tA[:, os_], lnbc[:, o:o + 1],
                               ALU.add)
                    dma(s_hbm[:, :, g0:g0 + 128].rearrange("c p t -> p c t"),
                        sTt[:, :].rearrange("p (c t) -> p c t", c=NC8))

    ph2 = contextlib.ExitStack()
    with ph2:
        lgb = ph2.enter_context(nc.sbuf_tensor([E, 128], BF16))
        ptm = ph2.enter_context(nc.sbuf_tensor([128, 8], F32))
        m1 = ph2.enter_context(nc.sbuf_tensor([128, 1], F32))
        m2 = ph2.enter_context(nc.sbuf_tensor([128, 1], F32))
        oh1 = ph2.enter_context(nc.sbuf_tensor([128, 8], F32))
        oh2 = ph2.enter_context(nc.sbuf_tensor([128, 8], F32))
        p2t = ph2.enter_context(nc.sbuf_tensor([128, 8], F32))
        ew1 = ph2.enter_context(nc.sbuf_tensor([128, 1], F32))
        ew2 = ph2.enter_context(nc.sbuf_tensor([128, 1], F32))
        W8 = ph2.enter_context(nc.sbuf_tensor([128, 8], F32))
        w8T = ph2.enter_context(nc.sbuf_tensor([8, 128], F32))
        for cc in range(NCC):
            g0 = cc * 128
            with nc.psum_tensor([E, 128], F32) as pl:
                for c in range(NC8):
                    mm(pl[:, :], gwT[:, c * 8:(c + 1) * 8],
                       xnT[:, c, g0:g0 + 128], c == 0, c == NC8 - 1)
                cp(lgb[:, :], pl[:, :])
            with nc.psum_tensor([128, 8], BF16) as plt:
                tp(plt[:, :], lgb[:, :], idb[0:8, 0:8])
                cp(ptm[:, :], plt[:, :])
            w(nc.vector)
            inc(nc.vector.reduce_max(m1[:, :], ptm[:, :], axis=AX.X))
            ts(oh1[:, :], ptm[:, :], m1[:, :], ALU.is_equal)
            ts(p2t[:, :], oh1[:, :], 1e30, ALU.mult)
            tt(p2t[:, :], ptm[:, :], p2t[:, :], ALU.subtract)
            w(nc.vector)
            inc(nc.vector.reduce_max(m2[:, :], p2t[:, :], axis=AX.X))
            ts(oh2[:, :], p2t[:, :], m2[:, :], ALU.is_equal)
            tt(ew1[:, :], m1[:, :], m2[:, :], ALU.subtract)
            act(ew1[:, :], ew1[:, :], AF.Sigmoid)
            tt(ew2[:, :], onef[:, 0:1], ew1[:, :], ALU.subtract)
            ts(oh1[:, :], oh1[:, :], ew1[:, :], ALU.mult)
            ts(oh2[:, :], oh2[:, :], ew2[:, :], ALU.mult)
            tt(W8[:, :], oh1[:, :], oh2[:, :], ALU.add)
            with nc.psum_tensor([8, 128], F32) as pw8:
                tp(pw8[:, :], W8[:, :], idf[:, :])
                cp(w8T[:, :], pw8[:, :])
            dma(w8h[:, g0:g0 + 128], w8T[:, :])

    ph3 = contextlib.ExitStack()
    with ph3:
        wgT = ph3.enter_context(nc.sbuf_tensor([128, NC8, FCH], BF16))
        wuT = ph3.enter_context(nc.sbuf_tensor([128, NC8, FCH], BF16))
        woT = ph3.enter_context(nc.sbuf_tensor([128, NFH, 1024], BF16))
        stg = ph3.enter_context(nc.sbuf_tensor([128, 1024], F32))
        stgb = ph3.enter_context(nc.sbuf_tensor([128, 1024], BF16))
        silu_t = ph3.enter_context(nc.sbuf_tensor([128, 512], F32))
        hall = ph3.enter_context(nc.sbuf_tensor([128, NFH, 512], BF16))
        wbs = ph3.enter_context(nc.sbuf_tensor([128, 512], F32))
        w8s = ph3.enter_context(nc.sbuf_tensor([1, 512], F32))
        gas = ph3.enter_context(nc.sbuf_tensor([128, NC8, 512], BF16))
        ctr = ph3.enter_context(nc.sbuf_tensor([128, 512], F32))
        for e in range(E):
            for fc in range(NFC):
                for (dst, src) in ((wgT, wig), (wuT, wiu)):
                    for fch in range(FCH // 128):
                        fr = fc * FCH + fch * 128
                        dma(stg[:, :], src[e, fr:fr + 128, :])
                        cp(stgb[:, :], stg[:, :])
                        with nc.psum_tensor([128, 1024], BF16) as ptp:
                            for dch in range(NC8):
                                tp(ptp[:, dch * 128:(dch + 1) * 128],
                                   stgb[:, dch * 128:(dch + 1) * 128],
                                   idb[:, :])
                            cp(dst[:, :, fch * 128:(fch + 1) * 128],
                               ptp[:, :].rearrange("p (c t) -> p c t", c=NC8))
                for dch in range(NC8):
                    dma(stg[:, 0:FCH],
                        wo_w[e, dch * 128:(dch + 1) * 128,
                             fc * FCH:(fc + 1) * FCH])
                    cp(stgb[:, 0:FCH], stg[:, 0:FCH])
                    with nc.psum_tensor([128, NFH * 128], BF16) as ptp:
                        for fh in range(NFH):
                            tp(ptp[:, fh * 128:(fh + 1) * 128],
                               stgb[:, fh * 128:(fh + 1) * 128], idb[:, :])
                        cp(woT[:, :, dch * 128:(dch + 1) * 128],
                           ptp[:, :].rearrange("p (c t) -> p c t", c=NFH))
                for tc in range(NSC):
                    t0 = tc * 512
                    with nc.psum_tensor([128, 512], F32) as pg, \
                            nc.psum_tensor([128, 512], F32) as pu:
                        for fo in range(NFH):
                            fs = slice(fo * 128, (fo + 1) * 128)
                            w(nc.tensor)
                            for dch in range(NC8):
                                mm(pg[:, :], wgT[:, dch, fs],
                                   xnT[:, dch, t0:t0 + 512],
                                   dch == 0, dch == NC8 - 1,
                                   n=(1 if dch == NC8 - 1 else 0), wait=False)
                            w(nc.tensor)
                            for dch in range(NC8):
                                mm(pu[:, :], wuT[:, dch, fs],
                                   xnT[:, dch, t0:t0 + 512],
                                   dch == 0, dch == NC8 - 1,
                                   n=(1 if dch == NC8 - 1 else 0), wait=False)
                            act(silu_t[:, :], pg[:, :], AF.Silu)
                            tt(hall[:, fo, :], silu_t[:, :], pu[:, :],
                               ALU.mult)
                    dma(w8s[0:1, :], w8h[e:e + 1, t0:t0 + 512])
                    with nc.psum_tensor([128, 512], F32) as pwb:
                        mm(pwb[:, :], onef[0:1, :], w8s[0:1, :],
                           True, True)
                        cp(wbs[:, :], pwb[:, :])
                    if not (e == 0 and fc == 0):
                        dma(gas[:, :, :],
                            gacc[:, :, t0:t0 + 512].rearrange(
                                "c p t -> p c t"))
                    with nc.psum_tensor([128, 512], F32) as peo:
                        for do in range(NC8):
                            w(nc.tensor)
                            for fh in range(NFH):
                                mm(peo[:, :], woT[:, fh, do * 128:(do + 1) * 128],
                                   hall[:, fh, :], fh == 0, fh == NFH - 1,
                                   n=(1 if fh == NFH - 1 else 0), wait=False)
                            tt(ctr[:, :], wbs[:, :], peo[:, :], ALU.mult)
                            if e == 0 and fc == 0:
                                cp(gas[:, do, :], ctr[:, :])
                            else:
                                tt(gas[:, do, :], gas[:, do, :], ctr[:, :],
                                   ALU.add)
                    dma(gacc[:, :, t0:t0 + 512].rearrange("c p t -> p c t"),
                        gas[:, :, :])

    ph4 = contextlib.ExitStack()
    with ph4:
        gat = ph4.enter_context(nc.sbuf_tensor([128, NC8, 128], BF16))
        gsl = ph4.enter_context(nc.sbuf_tensor([128, 1024], F32))
        ssl = ph4.enter_context(nc.sbuf_tensor([128, NC8, 128], BF16))
        fgt = ph4.enter_context(nc.sbuf_tensor([128, 1024], F32))
        xt2 = ph4.enter_context(nc.sbuf_tensor([128, 1024], F32))
        osb = ph4.enter_context(nc.sbuf_tensor([128, 1024], F32))
        for cc in range(NCC):
            g0 = cc * 128
            dma(gat[:, :, :],
                gacc[:, :, g0:g0 + 128].rearrange("c p t -> p c t"))
            dma(ssl[:, :, :],
                s_hbm[:, :, g0:g0 + 128].rearrange("c p t -> p c t"))
            dma(xt2[:, :], x[g0:g0 + 128, :])
            act(gsl[:, :], gat[:, :, :].rearrange("p c t -> p (c t)"),
                AF.Silu)
            tt(fgt[:, :], gsl[:, :],
               ssl[:, :, :].rearrange("p c t -> p (c t)"), ALU.mult)
            with nc.psum_tensor([128, 1024], F32) as pf:
                for c in range(NC8):
                    tp(pf[:, c * 128:(c + 1) * 128],
                       fgt[:, c * 128:(c + 1) * 128], idf[:, :])
                tt(osb[:, :], xt2[:, :], pf[:, :], ALU.add)
            dma(y_out[g0:g0 + 128, :], osb[:, :])
    w(nc.sync)
    st.close()
    return nc


def kernel(L0_attn_x, L0_attn_Wq, L0_attn_Wk, L0_attn_Wv, L0_attn_Wbeta,
           L0_attn_b_beta, L0_attn_Wo, L0_attn_b_o, L0_attn_ln_g,
           L0_attn_ln_b, L0_ffn_gate_w, L0_ffn_wi_gate, L0_ffn_wi_up,
           L0_ffn_wo):
    nc = build_nc()
    I = np.eye(128, dtype=np.float32)
    ut = np.triu(np.ones((128, 128), dtype=np.float32))
    mneg = np.where(ut > 0, 0.0, -1e30).astype(np.float32)
    mneg = np.tile(mneg, (1, 16))
    stc = np.zeros((16, 1024), dtype=np.float32)
    for c in range(8):
        for p in range(128):
            stc[(c * 128 + p) // 64, c * 128 + p] = 1.0
    sbc = np.zeros((128, 128), dtype=np.float32)
    for c in range(8):
        for j in range(128):
            sbc[j, c * 16 + (c * 128 + j) // 64] = 1.0
    import ml_dtypes
    bf = ml_dtypes.bfloat16
    ones = np.ones((128, 128), dtype=np.float32)
    base = {
        "Wq": np.ascontiguousarray(L0_attn_Wq, np.float32),
        "Wk": np.ascontiguousarray(L0_attn_Wk, np.float32),
        "Wv": np.ascontiguousarray(L0_attn_Wv, np.float32),
        "Wbeta": np.ascontiguousarray(L0_attn_Wbeta, np.float32),
        "bbeta": np.ascontiguousarray(L0_attn_b_beta, np.float32).reshape(16, 1),
        "Wo": np.ascontiguousarray(L0_attn_Wo, np.float32),
        "bo": np.ascontiguousarray(L0_attn_b_o, np.float32).reshape(8, 128, 1),
        "lng": np.ascontiguousarray(L0_attn_ln_g, np.float32).reshape(8, 128, 1),
        "lnb": np.ascontiguousarray(L0_attn_ln_b, np.float32).reshape(8, 128, 1),
        "gw": np.ascontiguousarray(L0_ffn_gate_w, np.float32),
        "wig": np.ascontiguousarray(L0_ffn_wi_gate, np.float32),
        "wiu": np.ascontiguousarray(L0_ffn_wi_up, np.float32),
        "wo_w": np.ascontiguousarray(L0_ffn_wo, np.float32),
        "I128f": I,
        "I128b": I.astype(bf),
        "UTb": ut.astype(bf),
        "mneg": mneg.astype(bf),
        "STb": stc.astype(bf),
        "Sb": sbc.astype(bf),
        "ones_f": ones,
    }
    xs = np.ascontiguousarray(np.asarray(L0_attn_x, np.float32))
    in_maps = []
    for c in range(8):
        m = dict(base)
        m["x"] = xs[c]
        in_maps.append(m)
    res = run_bass_kernel_spmd(nc, in_maps, core_ids=list(range(8)))
    out = np.stack([r["y"] for r in res.results], axis=0)
    return out.astype(np.float32)
